# version 01 (final)
# baseline (speedup 1.0000x reference)
import numpy as np
import ml_dtypes
import concourse.bass as bass
import concourse.mybir as mybir
from concourse.bass_utils import run_bass_kernel_spmd
from contextlib import ExitStack

F32 = mybir.dt.float32
BF16 = mybir.dt.bfloat16
AF = mybir.ActivationFunctionType
ALU = mybir.AluOpType
AX = mybir.AxisListType
BF = ml_dtypes.bfloat16

NCORE = 8
L = 16384
DM = 1024
DI = 2048
DS = 16
DSL = DI // NCORE
NDT = DSL // 128
TC = 1024
NCH = L // TC
TF = 512
NF = L // TF
NW = 2
NE = 8
NMH = 4
XSR = 3
RG = [list(range(NCORE))]

CV_NEG1, CV_INV1024, CV_EPS, CV_30, CV_NEG1E30, CV_ONE, CV_NEGHALF, CV_ZERO = range(8)


class Plan:
    def __init__(self):
        self.items = {e: [] for e in ("sync", "tensor", "scalar", "vector", "gpsimd")}
        self.cnt = {}

    def n(self, sem):
        return self.cnt.get(sem, 0)

    def op(self, eng, fn, wait=(), inc=()):
        w = [(s, v) for (s, v) in wait if v > 0]
        for s, k in inc:
            self.cnt[s] = self.cnt.get(s, 0) + k
        self.items[eng].append((w, fn, [(s, k) for (s, k) in inc if k > 0]))


def build_nc():
    nc = bass.Bass("TRN2", target_bir_lowering=False, debug=False)

    def din(name, shape, dt=F32):
        return nc.dram_tensor(name, shape, dt, kind="ExternalInput")

    xt = din("xt", [DM, L])
    xres = din("xres", [NW * TC, DM])
    win = din("win", [DM, 2 * DSL])
    wxp = din("wxp", [DSL, 48])
    wosl = din("wosl", [DSL, DM])
    alog = din("alog", [128, NDT * DS])
    dtwi = din("dtwi", [128, NDT])
    dtbi = din("dtbi", [128, NDT])
    dveci = din("dveci", [128, NDT])
    convwi = din("convwi", [128, NDT * 4])
    convbi = din("convbi", [128, NDT])
    wri = din("wri", [DM, NE])
    brbi = din("brbi", [128, NE])
    b1i = din("b1i", [128, NE * 16])
    b2pi = din("b2pi", [128, DM])
    w1 = din("w1", [NE, DM, DI])
    w2 = din("w2", [NE, DI, DM])
    selc = din("selc", [48, 17 * 128], BF16)
    selei = din("selei", [128, NE * 128], BF16)
    identi = din("identi", [128, 128], BF16)
    identfi = din("identfi", [128, 128])
    cvecin = din("cvecin", [128, 8])
    out_d = nc.dram_tensor("out", [NW * TC, DM], F32, kind="ExternalOutput")

    bcd_part = nc.dram_tensor("bcd_part", [NCH, 48, TC], F32)
    bcd_full = nc.dram_tensor("bcd_full", [NCH, 48, TC], F32)
    ops_in = nc.dram_tensor("ops_in", [NW, NE * DM, TC], BF16)
    ops_out = nc.dram_tensor("ops_out", [NW, DM, TC], BF16)
    x2tb_d = nc.dram_tensor("x2tb_d", [NW, TC, DM], BF16)
    x2tf_d = nc.dram_tensor("x2tf_d", [NW, TC, DM], F32)
    ffn_d = nc.dram_tensor("ffn_d", [NW, DM, TC], F32)

    es = ExitStack()
    with es:
        def sb(name, shape, dt=F32):
            return es.enter_context(nc.sbuf_tensor(name, shape, dt))

        def psum(name, shape):
            return es.enter_context(nc.psum_tensor(name, shape, F32))

        win_sb = sb("win_sb", [128, 8 * 512], BF16)
        wosl_sb = sb("wosl_sb", [128, NDT * DM], BF16)
        wxp_sb = sb("wxp_sb", [128, NDT * 48], BF16)
        wr_sb = sb("wr_sb", [128, 8 * NE], BF16)
        a_sb = sb("a_sb", [128, NDT * DS])
        dtw_sb = sb("dtw_sb", [128, NDT])
        dtb_sb = sb("dtb_sb", [128, NDT])
        dvec_sb = sb("dvec_sb", [128, NDT])
        convw_sb = sb("convw_sb", [128, NDT * 4])
        convb_sb = sb("convb_sb", [128, NDT])
        brb_sb = sb("brb_sb", [128, NE])
        b1_sb = sb("b1_sb", [128, NE * 16])
        b2p_sb = sb("b2p_sb", [128, DM], BF16)
        sel_sb = sb("sel_sb", [48, 17 * 128], BF16)
        sele_sb = sb("sele_sb", [128, NE * 128], BF16)
        ident_sb = sb("ident_sb", [128, 128], BF16)
        identf_sb = sb("identf_sb", [128, 128])
        cv = sb("cv", [128, 8])

        land = [sb(f"land{i}", [128, 512]) for i in range(2)]
        xtb = sb("xtb", [128, 8 * 512], BF16)
        xraw = [sb(f"xraw{i}", [128, NDT * (TF + 3)], BF16) for i in range(2)]
        cacc = sb("cacc", [128, NDT * TF])
        stmp1 = sb("stmp1", [128, TF])
        stmp2 = sb("stmp2", [128, TF])
        xsil = [sb(f"xsil{i}", [128, NDT * TF], BF16) for i in range(XSR)]
        zsil = [sb(f"zsil{i}", [128, NDT * TF], BF16) for i in range(XSR)]

        bcd_ld = sb("bcdld", [48, TF])
        bcd_bf = [sb(f"bcdbf{i}", [48, TC], BF16) for i in range(2)]
        bcd_pt = sb("bcdpt", [48, TF])

        vbuf = sb("vbuf", [128, TF])
        sp1 = sb("sp1", [128, TF])
        sp2 = sb("sp2", [128, TF])
        dt_sb = sb("dt_sb", [128, NDT * TC], BF16)
        u_sb = sb("u_sb", [128, NDT * TC], BF16)
        dAb = [sb(f"dA{i}", [128, TC]) for i in range(2)]
        Bbc = [sb(f"Bbc{i}", [128, TC], BF16) for i in range(2)]
        Cbc = [sb(f"Cbc{i}", [128, TC], BF16) for i in range(2)]
        dbx = [sb(f"dbx{i}", [128, TC], BF16) for i in range(2)]
        hbf = [sb(f"hbf{i}", [128, TC], BF16) for i in range(2)]
        hcb = [sb(f"hcb{i}", [128, TF], BF16) for i in range(6)]
        trA0 = sb("trA0", [128, NDT * TC], BF16)
        trA1 = sb("trA1", [128, NDT * TC], BF16)
        trB0 = sb("trB0", [128, NDT * TC], BF16)
        trB1 = sb("trB1", [128, NDT * TC], BF16)
        trC0 = sb("trC0", [128, NDT * TC], BF16)
        ygb = [sb(f"ygb{i}", [128, NDT * TC], BF16) for i in range(2)]
        carry = sb("carry", [128, NDT * DS])

        opst = sb("opst", [128, 2 * TC], BF16)

        x2f_sb = sb("x2f_sb", [128, 8 * TC], BF16)
        attn_t = sb("attn_t", [128, TC], BF16)
        xres_t = sb("xres_t", [128, DM])
        s_t = sb("s_t", [128, DM])
        x2t = sb("x2t", [128, DM])
        ffnt_t = sb("ffnt_t", [128, DM])
        lnst = sb("lnst", [128, 8])
        lgb = sb("lgb", [128, NE])
        eq1b = sb("eq1b", [128, NE])
        l2b = sb("l2b", [128, NE])
        eq2b = sb("eq2b", [128, NE])
        g1b = sb("g1b", [128, NE])
        gatepad = sb("gatepad", [128, 128])
        gfbpad = sb("gfbpad", [128, TC], BF16)
        gate_bc = sb("gate_bc", [128, TC], BF16)
        wld = [sb(f"wld{i}", [128, 512]) for i in range(2)]
        wbf = [sb(f"wbf{i}", [128, 1024], BF16) for i in range(2)]
        hg_sb = sb("hg_sb", [128, NMH * TC], BF16)
        hrel = [sb(f"hrel{i}", [128, 512], BF16) for i in range(2)]
        fstg = sb("fstg", [128, 2048])

        ps_in = psum("ps_in", [128, 512])
        ps_bcd = psum("ps_bcd", [48, 512])
        ps_bc = [psum("ps_bc0", [128, 512]), psum("ps_bc1", [128, 512])]
        ps_op = [psum("ps_op0", [128, 512]), psum("ps_op1", [128, 512])]
        ps_m = [psum("ps_m0", [128, 512]), psum("ps_m1", [128, 512])]

        sems = {}
        for s in ["sC", "sXT", "sBCDLD", "sLN1D", "sX2F", "sWL", "sOPST",
                  "sBCDP", "sX2D", "sFFA", "sLN2D", "sOUT",
                  "sXT0", "sXT1", "sOP0", "sOP1", "sX2B", "sX2G", "sW0", "sW1", "cXTB", "cINMM", "cINEV", "cXM", "cCONV", "cXPMM",
                  "cBCDEV", "cBCDBF", "cBCMM", "cBCF",
                  "cVC", "cSPL", "cDT", "cU", "cDA", "cDBX", "cSCAN",
                  "cCARRY", "cHC", "cYG", "cOPMM", "cOPEV", "ccBCD", "ccRS",
                  "cS", "cSQ", "cLN", "cX2", "cRT", "cGP", "cGT", "cGFB",
                  "cGBC", "cWC", "cMOMM", "cHG", "cMOEV", "cOUTT", "cCC", "cSQR"]:
            sems[s] = es.enter_context(nc.semaphore(s))
        SM = sems

        P = Plan()

        def ndc(buf, dt_, lo, hi):
            return buf[:, dt_ * TC + lo: dt_ * TC + hi]

        def ndf(buf, dt_, lo, hi):
            return buf[:, dt_ * TF + lo: dt_ * TF + hi]

        def lane(buf, dt_, half):
            return buf[:, dt_ * TC + half * TF: dt_ * TC + half * TF + TF]

        def xr(i, dt_, lo, hi):
            return xraw[i][:, dt_ * (TF + 3) + lo: dt_ * (TF + 3) + hi]

        def dload(dst_ap, src_ap):
            P.op("sync", lambda e, d=dst_ap, s=src_ap: e.dma_start(out=d, in_=s),
                 inc=[("sC", 16)])

        dload(sel_sb[:, :], selc.ap())
        dload(sele_sb[:, :], selei.ap())
        dload(ident_sb[:, :], identi.ap())
        dload(identf_sb[:, :], identfi.ap())
        dload(cv[:, :], cvecin.ap())
        dload(dtw_sb[:, :], dtwi.ap())
        dload(dtb_sb[:, :], dtbi.ap())
        dload(dvec_sb[:, :], dveci.ap())
        dload(convw_sb[:, :], convwi.ap())
        dload(convb_sb[:, :], convbi.ap())
        dload(brb_sb[:, :], brbi.ap())
        dload(b1_sb[:, :], b1i.ap())
        dload(a_sb[:, :], alog.ap())
        NDIRECT = 13

        cast_jobs = []
        winr = win.ap().rearrange("(a p) m -> p a m", p=128)
        winsbr = win_sb.ap().rearrange("p (a m) -> p a m", a=8)
        for piece in range(4):
            cast_jobs.append((winr[:, 2 * piece:2 * piece + 2, :],
                              s_t.ap().rearrange("p (a m) -> p a m", a=2),
                              winsbr[:, 2 * piece:2 * piece + 2, :]))
        woslr = wosl.ap().rearrange("(a p) m -> p a m", p=128)
        woslsbr = wosl_sb.ap().rearrange("p (a m) -> p a m", a=NDT)
        for piece in range(2):
            cast_jobs.append((woslr[:, piece:piece + 1, :],
                              s_t.ap().rearrange("p (a m) -> p a m", a=1),
                              woslsbr[:, piece:piece + 1, :]))
        cast_jobs.append((wxp.ap().rearrange("(a p) m -> p a m", p=128),
                          s_t.ap().rearrange("p (a m) -> p a m", a=2)[:, :, 0:48],
                          wxp_sb.ap().rearrange("p (a m) -> p a m", a=2)))
        cast_jobs.append((wri.ap().rearrange("(a p) m -> p a m", p=128),
                          s_t.ap().rearrange("p (a m) -> p a m", a=8)[:, :, 0:NE],
                          wr_sb.ap().rearrange("p (a m) -> p a m", a=8)))
        cast_jobs.append((b2pi.ap(), s_t.ap(), b2p_sb.ap()))

        for j, (src, st32, dbf) in enumerate(cast_jobs):
            P.op("sync", lambda e, d=st32, s=src: e.dma_start(out=d, in_=s),
                 wait=[("cCC", j)], inc=[("sC", 16)])
            P.op("scalar", lambda e, d=dbf, s=st32: e.activation(
                    out=d, in_=s, func=AF.Copy),
                 wait=[("sC", 16 * (NDIRECT + j + 1))], inc=[("cCC", 1)])
        NCONST = NDIRECT + len(cast_jobs)

        P.op("scalar", lambda e: e.activation(out=ffnt_t[:, 0:NDT * DS],
                                              in_=a_sb[:, :], func=AF.Exp),
             wait=[("sC", 16 * (NDIRECT + 1))], inc=[("cCC", 1)])
        P.op("vector", lambda e: e.tensor_scalar(
                out=a_sb[:, :], in0=ffnt_t[:, 0:NDT * DS],
                scalar1=cv[:, CV_NEG1:CV_NEG1 + 1], scalar2=None, op0=ALU.mult),
             wait=[("cCC", P.n("cCC")), ("sC", 16 * (NDIRECT + 1))], inc=[("cVC", 1)])
        P.op("gpsimd", lambda e: e.memset(gfbpad[:, :], 0.0), inc=[("cCONV", 1)])
        P.op("gpsimd", lambda e: e.memset(gatepad[:, :], 0.0), inc=[("cCONV", 1)])
        P.op("gpsimd", lambda e: (e.memset(xraw[0][:, :], 0.0),
                                  e.memset(xraw[1][:, :], 0.0))[-1],
             inc=[("cCONV", 1)])
        CPRO = P.n("cCONV")

        def front(f):
            c = f // 2
            for kt in range(8):
                idx = f * 8 + kt
                wl = [("sC", 16 * NCONST)] if idx == 0 else []
                if idx >= 2:
                    wl.append(("cXTB", idx - 1))
                sx = "sXT0" if idx % 2 == 0 else "sXT1"
                P.op("sync", lambda e, kt=kt, f=f, i=idx: e.dma_start(
                        out=land[i % 2][:, :],
                        in_=xt.ap()[kt * 128:(kt + 1) * 128, f * TF:(f + 1) * TF]),
                     wait=wl, inc=[(sx, 16)])
                wl2 = [(sx, P.n(sx))]
                if f >= 1 and kt == 0:
                    wl2.append(("cINMM", 4 * f))
                P.op("scalar", lambda e, kt=kt, i=idx: e.activation(
                        out=xtb[:, kt * 512:(kt + 1) * 512], in_=land[i % 2][:, :],
                        func=AF.Copy),
                     wait=wl2, inc=[("cXTB", 1)])
            for mt in range(4):
                u = f * 4 + mt

                def mmf(e, f=f, mt=mt):
                    r = None
                    for kt in range(8):
                        r = e.matmul(
                            ps_in[:, :],
                            win_sb[:, kt * 512 + mt * 128: kt * 512 + (mt + 1) * 128],
                            xtb[:, kt * 512:(kt + 1) * 512],
                            start=(kt == 0), stop=(kt == 7))
                    return r
                mmw = [("cXTB", f * 8 + 8), ("cINEV", u)]
                if f == 0 and mt == 0:
                    mmw.append(("cCC", 9))
                P.op("tensor", mmf, wait=mmw, inc=[("cINMM", 1)])
                if mt < 2:
                    P.op("scalar", lambda e, f=f, mt=mt: e.activation(
                            out=xr(f % 2, mt, 3, 3 + TF), in_=ps_in[:, :],
                            func=AF.Copy),
                         wait=[("cINMM", u + 1), ("cCONV", CPRO + max(0, f - 1))],
                         inc=[("cINEV", 1)])
                else:
                    dt_ = mt - 2
                    zw = [("cINMM", u + 1)]
                    if dt_ == 0:
                        zw += [("cINEV", max(0, 4 * f)), ("cXM", max(0, 2 * f))]
                    else:
                        zw += [("cINEV", 4 * f + 3)]
                    P.op("scalar", lambda e: e.activation(
                            out=stmp1[:, :], in_=ps_in[:, :], func=AF.Exp,
                            scale=cv[:, CV_NEG1:CV_NEG1 + 1]),
                         wait=zw)
                    P.op("scalar", lambda e: e.activation(
                            out=stmp2[:, :], in_=stmp1[:, :],
                            func=AF.Ln, bias=cv[:, CV_ONE:CV_ONE + 1]))
                    P.op("scalar", lambda e: e.activation(
                            out=stmp1[:, :], in_=stmp2[:, :],
                            func=AF.Exp, scale=cv[:, CV_NEG1:CV_NEG1 + 1]),
                         inc=[("cSPL", 1)])
                    P.op("vector", lambda e, f=f, dt_=dt_: e.tensor_tensor(
                            out=ndf(zsil[f % XSR], dt_, 0, TF), in0=ps_in[:, :],
                            in1=stmp1[:, :], op=ALU.mult),
                         wait=[("cSPL", P.n("cSPL"))],
                         inc=[("cINEV", 1)])

            def conv(e, f=f):
                r = None
                for dt_ in range(NDT):
                    r = e.tensor_scalar(out=ndf(cacc, dt_, 0, TF),
                                        in0=xr(f % 2, dt_, 0, TF),
                                        scalar1=convw_sb[:, dt_ * 4:dt_ * 4 + 1],
                                        scalar2=None, op0=ALU.mult)
                    for j in range(1, 4):
                        e.drain()
                        r = e.scalar_tensor_tensor(
                            out=ndf(cacc, dt_, 0, TF), in0=xr(f % 2, dt_, j, j + TF),
                            scalar=convw_sb[:, dt_ * 4 + j:dt_ * 4 + j + 1],
                            in1=ndf(cacc, dt_, 0, TF), op0=ALU.mult, op1=ALU.add)
                    e.drain()
                    r = e.tensor_scalar(out=ndf(cacc, dt_, 0, TF),
                                        in0=ndf(cacc, dt_, 0, TF),
                                        scalar1=convb_sb[:, dt_:dt_ + 1],
                                        scalar2=None, op0=ALU.add)
                    e.drain()
                for dt_ in range(NDT):
                    r = e.tensor_copy(out=xr((f + 1) % 2, dt_, 0, 3),
                                      in_=xr(f % 2, dt_, TF, TF + 3))
                return r
            P.op("vector", conv,
                 wait=[("cINEV", 4 * f + 2), ("cXM", max(0, 2 * f - 2))] +
                      ([("cCONV", CPRO)] if f == 0 else []),
                 inc=[("cCONV", 1)])
            for dt_ in range(NDT):
                xw = [("cCONV", P.n("cCONV")), ("cINEV", 4 * f + 4)]
                if dt_ == 1:
                    xw.append(("cXM", 2 * f + 1))
                P.op("scalar", lambda e: e.activation(
                        out=stmp1[:, :], in_=ndf(cacc, dt_, 0, TF) if False else
                        cacc[:, dt_ * TF:(dt_ + 1) * TF],
                        func=AF.Exp, scale=cv[:, CV_NEG1:CV_NEG1 + 1]) if False else
                     e.activation(
                        out=stmp1[:, :], in_=cacc[:, dt_ * TF:(dt_ + 1) * TF],
                        func=AF.Exp, scale=cv[:, CV_NEG1:CV_NEG1 + 1]),
                     wait=xw)
                P.op("scalar", lambda e: e.activation(
                        out=stmp2[:, :], in_=stmp1[:, :],
                        func=AF.Ln, bias=cv[:, CV_ONE:CV_ONE + 1]))
                P.op("scalar", lambda e: e.activation(
                        out=stmp1[:, :], in_=stmp2[:, :],
                        func=AF.Exp, scale=cv[:, CV_NEG1:CV_NEG1 + 1]),
                     inc=[("cSPL", 1)])
                P.op("vector", lambda e, f=f, dt_=dt_: e.tensor_tensor(
                        out=ndf(xsil[f % XSR], dt_, 0, TF),
                        in0=cacc[:, dt_ * TF:(dt_ + 1) * TF], in1=stmp1[:, :],
                        op=ALU.mult),
                     wait=[("cSPL", P.n("cSPL"))] +
                          ([("cXPMM", max(0, f - (XSR - 1)))] if dt_ == 0 else []),
                     inc=[("cXM", 1)])

            def xproj(e, f=f):
                r = None
                for dt_ in range(NDT):
                    r = e.matmul(ps_bcd[:, :],
                                 wxp_sb[:, dt_ * 48:(dt_ + 1) * 48],
                                 ndf(xsil[f % XSR], dt_, 0, TF),
                                 start=(dt_ == 0), stop=(dt_ == NDT - 1))
                return r
            P.op("tensor", xproj,
                 wait=[("cXM", P.n("cXM")), ("cBCDEV", f)],
                 inc=[("cXPMM", 1)])
            P.op("scalar", lambda e, f=f: e.copy(out=bcd_pt[:, :],
                                                 in_=ps_bcd[:, :]),
                 wait=[("cXPMM", P.n("cXPMM")), ("sBCDP", 16 * f)],
                 inc=[("cBCDEV", 1)])
            P.op("scalar", lambda e, f=f, c=c: e.dma_start(
                    out=bcd_part.ap()[c, :, (f % 2) * TF:(f % 2) * TF + TF],
                    in_=bcd_pt[:, :]),
                 inc=[("sBCDP", 16)])

        def scan_chunk(c):
            for h in range(2):
                P.op("sync", lambda e, c=c, h=h: e.dma_start(
                        out=bcd_ld[:, :],
                        in_=bcd_full.ap()[c, :, h * TF:(h + 1) * TF]),
                     wait=[("ccBCD", c + 1), ("cBCDBF", 2 * c + h)],
                     inc=[("sBCDLD", 16)])
                P.op("scalar", lambda e, c=c, h=h: e.activation(
                        out=bcd_bf[c % 2][:, h * TF:(h + 1) * TF],
                        in_=bcd_ld[:, :], func=AF.Copy),
                     wait=[("sBCDLD", 16 * (2 * c + h + 1))], inc=[("cBCDBF", 1)])

            jctr = [0]

            def bc_mm(row, c=c):
                outs = []
                for half in range(2):
                    j = jctr[0]
                    jctr[0] += 1
                    bank = j % 2

                    def mm(e, bank=bank, row=row, half=half, c=c):
                        if 16 <= row < 32:
                            rhs = bcd_bf[c % 2][32:48, half * 512:half * 512 + 512]
                            lhsT = sel_sb[32:48, (row - 16) * 128:(row - 15) * 128]
                        elif row == 32:
                            rhs = bcd_bf[c % 2][0:32, half * 512:half * 512 + 512]
                            lhsT = sel_sb[0:32, 16 * 128:17 * 128]
                        else:
                            rhs = bcd_bf[c % 2][0:32, half * 512:half * 512 + 512]
                            lhsT = sel_sb[0:32, row * 128:(row + 1) * 128]
                        return e.matmul(ps_bc[bank][:, :], lhsT, rhs,
                                        start=True, stop=True)
                    P.op("tensor", mm,
                         wait=[("cBCDBF", 2 * c + 2), ("cBCF", 66 * c + j - 1)],
                         inc=[("cBCMM", 1)])
                    outs.append((bank, half, 66 * c + j))
                return outs

            r_items = bc_mm(32)
            for dt_ in range(NDT):
                for (bank, half, j) in r_items:
                    frees = []
                    if dt_ == NDT - 1:
                        frees = [("cBCF", 1)]
                    P.op("vector", lambda e, bank=bank, dt_=dt_: e.tensor_scalar(
                            out=vbuf[:, :], in0=ps_bc[bank][:, :],
                            scalar1=dtw_sb[:, dt_:dt_ + 1],
                            scalar2=dtb_sb[:, dt_:dt_ + 1],
                            op0=ALU.mult, op1=ALU.add),
                         wait=[("cBCMM", j + 1), ("cDT", P.n("cDT")),
                               ("cDA", 32 * c)],
                         inc=frees)
                    P.op("vector", lambda e: e.tensor_scalar(
                            out=sp1[:, :], in0=vbuf[:, :],
                            scalar1=cv[:, CV_30:CV_30 + 1], scalar2=None,
                            op0=ALU.min),
                         inc=[("cVC", 1)])
                    P.op("scalar", lambda e: e.activation(out=sp2[:, :],
                                                          in_=sp1[:, :], func=AF.Exp),
                         wait=[("cVC", P.n("cVC"))])
                    P.op("scalar", lambda e: e.activation(out=sp1[:, :],
                                                          in_=sp2[:, :], func=AF.Ln,
                                                          bias=cv[:, CV_ONE:CV_ONE + 1]),
                         inc=[("cSPL", 1)])
                    P.op("vector", lambda e, dt_=dt_, half=half: e.tensor_tensor(
                            out=ndc(dt_sb, dt_, half * TF, half * TF + TF),
                            in0=sp1[:, :], in1=vbuf[:, :], op=ALU.max),
                         wait=[("cSPL", P.n("cSPL"))],
                         inc=[("cDT", 1)])
                    f = 2 * c + half
                    P.op("vector", lambda e, dt_=dt_, half=half, f=f: e.tensor_tensor(
                            out=ndc(u_sb, dt_, half * TF, half * TF + TF),
                            in0=ndc(dt_sb, dt_, half * TF, half * TF + TF),
                            in1=ndf(xsil[f % XSR], dt_, 0, TF), op=ALU.mult),
                         wait=[("cDBX", 32 * c)],
                         inc=[("cU", 1)])

            for n in range(DS):
                b_items = bc_mm(n)
                for (bank, half, j) in b_items:
                    P.op("scalar", lambda e, bank=bank, half=half, n=n: e.activation(
                            out=Bbc[n % 2][:, half * 512:half * 512 + 512],
                            in_=ps_bc[bank][:, :], func=AF.Copy),
                         wait=[("cBCMM", j + 1),
                               ("cDBX", max(0, 32 * c + 2 * n - 2))],
                         inc=[("cBCF", 1)])
                c_items = bc_mm(16 + n)
                for (bank, half, j) in c_items:
                    P.op("scalar", lambda e, bank=bank, half=half, n=n: e.activation(
                            out=Cbc[n % 2][:, half * 512:half * 512 + 512],
                            in_=ps_bc[bank][:, :], func=AF.Copy),
                         wait=[("cBCMM", j + 1),
                               ("cHC", max(0, 64 * c + 4 * n - 4))],
                         inc=[("cBCF", 1)])
                for dt_ in range(NDT):
                    idx = 32 * c + 2 * n + dt_
                    rg2 = idx % 2
                    P.op("scalar", lambda e, dt_=dt_, n=n, rg2=rg2: e.activation(
                            out=dAb[rg2][:, :], in_=ndc(dt_sb, dt_, 0, TC),
                            func=AF.Exp,
                            scale=a_sb[:, dt_ * DS + n:dt_ * DS + n + 1]),
                         wait=[("cDT", 4 * c + 4), ("cSCAN", max(0, idx - 1))],
                         inc=[("cDA", 1)])
                    P.op("gpsimd", lambda e, dt_=dt_, n=n, rg2=rg2: e.tensor_tensor(
                            out=dbx[rg2][:, :], in0=ndc(u_sb, dt_, 0, TC),
                            in1=Bbc[n % 2][:, :], op=ALU.mult),
                         wait=[("cU", 4 * c + 4), ("cBCF", 66 * c + 4 * n + 4),
                               ("cSCAN", max(0, idx - 1))],
                         inc=[("cDBX", 1)])
                    init = 0.0 if c == 0 else \
                        carry[:, dt_ * DS + n:dt_ * DS + n + 1]
                    P.op("vector", lambda e, rg2=rg2, init=init: e.tensor_tensor_scan(
                            out=hbf[rg2][:, :], data0=dAb[rg2][:, :],
                            data1=dbx[rg2][:, :], initial=init,
                            op0=ALU.mult, op1=ALU.add),
                         wait=[("cDA", P.n("cDA")), ("cDBX", P.n("cDBX")),
                               ("cCARRY", max(0, idx - 1))],
                         inc=[("cSCAN", 1)])
                    P.op("scalar", lambda e, rg2=rg2, dt_=dt_, n=n: e.copy(
                            out=carry[:, dt_ * DS + n:dt_ * DS + n + 1],
                            in_=hbf[rg2][:, TC - 1:TC]),
                         wait=[("cSCAN", P.n("cSCAN"))],
                         inc=[("cCARRY", 1)])
                    for half in range(2):
                        hs = (4 * n + 2 * dt_ + half) % 6
                        P.op("vector", lambda e, rg2=rg2, n=n, hs=hs, half=half:
                             e.tensor_tensor(
                                out=hcb[hs][:, :],
                                in0=hbf[rg2][:, half * 512:half * 512 + 512],
                                in1=Cbc[n % 2][:, half * 512:half * 512 + 512],
                                op=ALU.mult),
                             wait=[("cBCF", 66 * c + 4 * n + 6)],
                             inc=[("cHC", 1)])
                if n % 2 == 1:
                    dstA = trA0 if (n // 2) % 2 == 0 else trA1
                    for dt_ in range(NDT):
                        for half in range(2):
                            s0 = (4 * (n - 1) + 2 * dt_ + half) % 6
                            s1 = (4 * n + 2 * dt_ + half) % 6
                            P.op("vector", lambda e, s0=s0, s1=s1, d=dstA, dt_=dt_,
                                 half=half: e.tensor_tensor(
                                    out=lane(d, dt_, half), in0=hcb[s0][:, :],
                                    in1=hcb[s1][:, :], op=ALU.add))
                    if n % 4 == 3:
                        dstB = trB0 if (n // 4) % 2 == 0 else trB1
                        for dt_ in range(NDT):
                            for half in range(2):
                                P.op("vector", lambda e, d=dstB, dt_=dt_, half=half:
                                     e.tensor_tensor(
                                        out=lane(d, dt_, half),
                                        in0=lane(trA0, dt_, half),
                                        in1=lane(trA1, dt_, half), op=ALU.add))
                    if n == 7:
                        for dt_ in range(NDT):
                            for half in range(2):
                                P.op("vector", lambda e, dt_=dt_, half=half:
                                     e.tensor_tensor(
                                        out=lane(trC0, dt_, half),
                                        in0=lane(trB0, dt_, half),
                                        in1=lane(trB1, dt_, half), op=ALU.add))
                    if n == 15:
                        for dt_ in range(NDT):
                            for half in range(2):
                                P.op("vector", lambda e, dt_=dt_, half=half:
                                     e.tensor_tensor(
                                        out=lane(trA0, dt_, half),
                                        in0=lane(trB0, dt_, half),
                                        in1=lane(trB1, dt_, half), op=ALU.add))
                                P.op("vector", lambda e, dt_=dt_, half=half:
                                     e.tensor_tensor(
                                        out=lane(trB0, dt_, half),
                                        in0=lane(trC0, dt_, half),
                                        in1=lane(trA0, dt_, half), op=ALU.add))
            for dt_ in range(NDT):
                for half in range(2):
                    f = 2 * c + half
                    P.op("vector", lambda e, dt_=dt_, half=half, f=f:
                         e.scalar_tensor_tensor(
                            out=lane(trA1, dt_, half),
                            in0=ndf(xsil[f % XSR], dt_, 0, TF),
                            scalar=dvec_sb[:, dt_:dt_ + 1],
                            in1=lane(trB0, dt_, half),
                            op0=ALU.mult, op1=ALU.add))
                    P.op("vector", lambda e, dt_=dt_, half=half, f=f, c=c:
                         e.tensor_tensor(
                            out=lane(ygb[c % 2], dt_, half),
                            in0=lane(trA1, dt_, half),
                            in1=ndf(zsil[f % XSR], dt_, 0, TF), op=ALU.mult),
                         wait=[("cOPMM", max(0, 16 * (c - 1)))])
            P.op("vector", lambda e: e.engine_nop(), inc=[("cYG", 1)])

            for item in range(16):
                m, half = item // 2, item % 2

                def opmm(e, c=c, m=m, half=half, item=item):
                    r = None
                    for kt in range(NDT):
                        r = e.matmul(
                            ps_op[item % 2][:, :],
                            wosl_sb[:, kt * DM + m * 128: kt * DM + (m + 1) * 128],
                            ndc(ygb[c % 2], kt, half * 512, half * 512 + 512),
                            start=(kt == 0), stop=(kt == NDT - 1))
                    return r
                P.op("tensor", opmm,
                     wait=[("cYG", c + 1), ("cOPEV", max(0, 16 * c + item - 1))],
                     inc=[("cOPMM", 1)])
                pr = m // 2
                evw = [("cOPMM", P.n("cOPMM"))]
                if item % 4 == 0:
                    evw.append(("sOP0", P.n("sOP0")))
                P.op("scalar", lambda e, m=m, half=half, item=item: e.activation(
                        out=opst[:, (m % 2) * TC + half * 512:
                                 (m % 2) * TC + half * 512 + 512],
                        in_=ps_op[item % 2][:, :], func=AF.Copy),
                     wait=evw,
                     inc=[("cOPEV", 1)])
                if item % 4 == 3:
                    w = c // 8
                    sop = "sOP0"
                    P.op("scalar", lambda e, c=c, w=w, pr=pr: e.dma_start(
                            out=ops_in.ap()[w, (c % 8) * DM + pr * 256:
                                            (c % 8) * DM + (pr + 1) * 256, :]
                                .rearrange("(a p) t -> p a t", p=128),
                            in_=opst.ap().rearrange("p (a t) -> p a t", a=2)),
                         inc=[(sop, 16)])

        def ln_stats(buf):
            P.op("vector", lambda e: e.reduce_sum(out=lnst[:, 0:1], in_=buf[:, :],
                                                  axis=AX.X))
            P.op("scalar", lambda e: e.activation(out=ffnt_t[:, :], in_=buf[:, :],
                                                  func=AF.Square),
                 wait=[("cS", P.n("cS")), ("cSQR", P.n("cSQR"))],
                 inc=[("cSQ", 1)])
            P.op("vector", lambda e: e.reduce_sum(out=lnst[:, 1:2], in_=ffnt_t[:, :],
                                                  axis=AX.X),
                 wait=[("cSQ", P.n("cSQ"))],
                 inc=[("cSQR", 1)])
            P.op("vector", lambda e: e.tensor_scalar(
                    out=lnst[:, 2:3], in0=lnst[:, 0:1],
                    scalar1=cv[:, CV_INV1024:CV_INV1024 + 1], scalar2=None,
                    op0=ALU.mult))
            P.op("vector", lambda e: e.tensor_tensor(
                    out=lnst[:, 3:4], in0=lnst[:, 2:3], in1=lnst[:, 2:3],
                    op=ALU.mult))
            P.op("vector", lambda e: e.scalar_tensor_tensor(
                    out=lnst[:, 4:5], in0=lnst[:, 1:2],
                    scalar=cv[:, CV_INV1024:CV_INV1024 + 1], in1=lnst[:, 3:4],
                    op0=ALU.mult, op1=ALU.subtract),
                 inc=[("cLN", 1)])
            P.op("scalar", lambda e: e.activation(out=lnst[:, 5:6], in_=lnst[:, 4:5],
                                                  func=AF.Ln,
                                                  bias=cv[:, CV_EPS:CV_EPS + 1]),
                 wait=[("cLN", P.n("cLN"))])
            P.op("scalar", lambda e: e.activation(
                    out=lnst[:, 6:7], in_=lnst[:, 5:6], func=AF.Exp,
                    scale=cv[:, CV_NEGHALF:CV_NEGHALF + 1]),
                 inc=[("cLN", 1)])

        def wave_pre(w):
            for tt in range(8):
                k = w * 8 + tt
                P.op("sync", lambda e, w=w, tt=tt: e.dma_start(
                        out=attn_t[:, :],
                        in_=ops_out.ap()[w, :, tt * 128:(tt + 1) * 128],
                        transpose=True),
                     wait=[("ccRS", w + 1), ("cS", P.n("cS")),
                           ("sX2B", P.n("sX2B"))],
                     inc=[("sLN1D", 16)])
                P.op("sync", lambda e, k=k: e.dma_start(
                        out=xres_t[:, :], in_=xres.ap()[k * 128:(k + 1) * 128, :]),
                     inc=[("sLN1D", 16)])
                P.op("vector", lambda e: e.tensor_tensor(
                        out=s_t[:, :], in0=attn_t[:, :], in1=xres_t[:, :],
                        op=ALU.add),
                     wait=[("sLN1D", 16 * P.n("sLN1D") // 16),
                           ("cX2", max(0, k - 0)) if False else
                           ("cX2", P.n("cX2"))],
                     inc=[("cS", 1)])
                ln_stats(s_t)
                P.op("vector", lambda e, k=k: e.tensor_scalar(
                        out=x2t[:, :], in0=s_t[:, :],
                        scalar1=lnst[:, 2:3], scalar2=lnst[:, 6:7],
                        op0=ALU.subtract, op1=ALU.mult),
                     wait=[("cLN", P.n("cLN")), ("sX2G", P.n("sX2G")),
                           ("sOUT", P.n("sOUT"))],
                     inc=[("cX2", 1)])
                P.op("scalar", lambda e: e.activation(
                        out=attn_t[:, 0:DM], in_=x2t[:, :], func=AF.Copy),
                     wait=[("cX2", P.n("cX2")), ("sX2B", P.n("sX2B"))])
                P.op("scalar", lambda e, w=w, tt=tt: e.dma_start(
                        out=x2tb_d.ap()[w, tt * 128:(tt + 1) * 128, :],
                        in_=attn_t[:, 0:DM]),
                     inc=[("sX2B", 16)])
                P.op("scalar", lambda e, w=w, tt=tt: e.dma_start(
                        out=x2tf_d.ap()[w, tt * 128:(tt + 1) * 128, :],
                        in_=x2t[:, :]),
                     inc=[("sX2G", 16)])
            for kt in range(8):
                P.op("sync", lambda e, w=w, kt=kt: e.dma_start(
                        out=x2f_sb[:, kt * TC:(kt + 1) * TC],
                        in_=x2tb_d.ap()[w, :, kt * 128:(kt + 1) * 128],
                        transpose=True),
                     wait=[("sX2B", 16 * 8 * (w + 1)), ("cMOMM", 768 * w)],
                     inc=[("sX2F", 16)])
            for tt in range(8):
                k = w * 8 + tt

                def rtmm(e, tt=tt):
                    r = None
                    for kt in range(8):
                        r = e.matmul(ps_m[0][:, 0:NE],
                                     x2f_sb[:, kt * TC + tt * 128:
                                            kt * TC + (tt + 1) * 128],
                                     wr_sb[:, kt * NE:(kt + 1) * NE],
                                     start=(kt == 0), stop=(kt == 7))
                    return r
                P.op("tensor", rtmm,
                     wait=[("sX2F", 16 * 8 * (w + 1)), ("cGP", k),
                           ("cMOEV", P.n("cMOEV"))],
                     inc=[("cRT", 1)])
                P.op("vector", lambda e: e.tensor_tensor(
                        out=lgb[:, :], in0=ps_m[0][:, 0:NE], in1=brb_sb[:, :],
                        op=ALU.add),
                     wait=[("cRT", P.n("cRT"))],
                     inc=[("cGP", 1)])
                P.op("vector", lambda e: e.reduce_max(out=lnst[:, 0:1],
                                                      in_=lgb[:, :], axis=AX.X))
                P.op("vector", lambda e: e.tensor_scalar(
                        out=eq1b[:, :], in0=lgb[:, :], scalar1=lnst[:, 0:1],
                        scalar2=None, op0=ALU.is_ge))
                P.op("vector", lambda e: e.scalar_tensor_tensor(
                        out=l2b[:, :], in0=eq1b[:, :],
                        scalar=cv[:, CV_NEG1E30:CV_NEG1E30 + 1], in1=lgb[:, :],
                        op0=ALU.mult, op1=ALU.add))
                P.op("vector", lambda e: e.reduce_max(out=lnst[:, 1:2],
                                                      in_=l2b[:, :], axis=AX.X))
                P.op("vector", lambda e: e.tensor_scalar(
                        out=eq2b[:, :], in0=l2b[:, :], scalar1=lnst[:, 1:2],
                        scalar2=None, op0=ALU.is_ge))
                P.op("vector", lambda e: e.tensor_tensor(
                        out=lnst[:, 2:3], in0=lnst[:, 1:2], in1=lnst[:, 0:1],
                        op=ALU.subtract),
                     inc=[("cLN", 1)])
                P.op("scalar", lambda e: e.activation(out=lnst[:, 3:4],
                                                      in_=lnst[:, 2:3],
                                                      func=AF.Exp),
                     wait=[("cLN", P.n("cLN"))],
                     inc=[("cSQ", 1)])
                P.op("vector", lambda e: e.tensor_scalar(
                        out=lnst[:, 4:5], in0=lnst[:, 3:4],
                        scalar1=cv[:, CV_ONE:CV_ONE + 1], scalar2=None,
                        op0=ALU.add),
                     wait=[("cSQ", P.n("cSQ"))])
                P.op("vector", lambda e: e.reciprocal(out=lnst[:, 5:6],
                                                      in_=lnst[:, 4:5]))
                P.op("vector", lambda e: e.tensor_scalar(
                        out=lnst[:, 6:7], in0=lnst[:, 5:6],
                        scalar1=cv[:, CV_NEG1:CV_NEG1 + 1],
                        scalar2=cv[:, CV_ONE:CV_ONE + 1],
                        op0=ALU.mult, op1=ALU.add))
                P.op("vector", lambda e: e.tensor_scalar(
                        out=g1b[:, :], in0=eq1b[:, :], scalar1=lnst[:, 5:6],
                        scalar2=None, op0=ALU.mult))
                P.op("vector", lambda e: e.scalar_tensor_tensor(
                        out=gatepad[:, 0:NE], in0=eq2b[:, :], scalar=lnst[:, 6:7],
                        in1=g1b[:, :], op0=ALU.mult, op1=ALU.add),
                     wait=[("cCONV", CPRO)] if (w == 0 and tt == 0) else [],
                     inc=[("cLN", 1)])
                P.op("tensor", lambda e: e.transpose(ps_m[1][:, 0:128],
                                                     gatepad[:, :], identf_sb[:, :]),
                     wait=[("cLN", P.n("cLN")), ("cGFB", k),
                           ("cMOEV", P.n("cMOEV"))],
                     inc=[("cGT", 1)])
                P.op("scalar", lambda e, tt=tt: e.copy(
                        out=gfbpad[0:8, tt * 128:(tt + 1) * 128],
                        in_=ps_m[1][0:8, 0:128]),
                     wait=[("cGT", P.n("cGT"))] +
                          ([("cCONV", CPRO)] if (w == 0 and tt == 0) else []),
                     inc=[("cGFB", 1)])

        def wave_expert(w, e_i):
            hg_prev = P.n("cHG")
            for half in range(2):
                P.op("tensor", lambda e, e_i=e_i, half=half: e.matmul(
                        ps_m[1][:, :], sele_sb[:, e_i * 128:(e_i + 1) * 128],
                        gfbpad[:, half * 512:half * 512 + 512],
                        start=True, stop=True),
                     wait=[("cGFB", 8 * (w + 1)), ("cMOEV", P.n("cMOEV")),
                           ("cHG", hg_prev), ("cGBC", P.n("cGBC"))],
                     inc=[("cGT", 1)])
                P.op("scalar", lambda e, half=half: e.activation(
                        out=gate_bc[:, half * 512:half * 512 + 512],
                        in_=ps_m[1][:, :], func=AF.Copy),
                     wait=[("cGT", P.n("cGT"))],
                     inc=[("cGBC", 1)])
            for mh in range(NMH):
                hg_base = P.n("cHG")
                for ms4 in range(4):
                    ms = mh * 4 + ms4
                    widx = P.n("cWC") // 2
                    for q in range(2):
                        swl = "sW0" if q == 0 else "sW1"
                        P.op("sync", lambda e, e_i=e_i, ms=ms, q=q: e.dma_start(
                                out=wld[q].ap().rearrange("p (a m) -> p a m", a=4),
                                in_=w1.ap()[e_i, q * 512:(q + 1) * 512,
                                            ms * 128:(ms + 1) * 128]
                                    .rearrange("(a p) m -> p a m", p=128)),
                             wait=[("cWC", max(0, 2 * widx - 1)) if q == 0 else
                                   ("cWC", max(0, 2 * widx))],
                             inc=[(swl, 16)])
                    for q in range(2):
                        swl = "sW0" if q == 0 else "sW1"
                        P.op("scalar", lambda e, q=q, widx=widx: e.activation(
                                out=wbf[widx % 2][:, q * 512:(q + 1) * 512],
                                in_=wld[q][:, :], func=AF.Copy),
                             wait=[(swl, P.n(swl)),
                                   ("cMOMM", max(0, P.n("cMOMM") - 2))
                                   if q == 0 else (swl, P.n(swl))],
                             inc=[("cWC", 1)])
                    for half in range(2):
                        pb = (2 * ms4 + half) % 2

                        def w1mm(e, widx=widx, half=half, pb=pb):
                            r = None
                            for kt in range(8):
                                r = e.matmul(
                                    ps_m[pb][:, :],
                                    wbf[widx % 2][:, kt * 128:(kt + 1) * 128],
                                    x2f_sb[:, kt * TC + half * 512:
                                           kt * TC + half * 512 + 512],
                                    start=(kt == 0), stop=(kt == 7))
                            return r
                        P.op("tensor", w1mm,
                             wait=[("cWC", 2 * widx + 2),
                                   ("cMOEV", max(0, P.n("cMOEV") - 1)),
                                   ("cGBC", P.n("cGBC")),
                                   ("sX2F", 16 * 8 * (w + 1))],
                             inc=[("cMOMM", 1)])
                        P.op("scalar", lambda e, pb=pb, e_i=e_i, ms=ms, half=half:
                             e.activation(
                                out=hrel[half][:, :], in_=ps_m[pb][:, :],
                                func=AF.Relu,
                                bias=b1_sb[:, e_i * 16 + ms:e_i * 16 + ms + 1]),
                             wait=[("cMOMM", P.n("cMOMM")),
                                   ("cHG", max(0, P.n("cHG") - 1))],
                             inc=[("cMOEV", 1)])
                        P.op("vector", lambda e, ms4=ms4, half=half: e.tensor_tensor(
                                out=hg_sb[:, ms4 * TC + half * 512:
                                          ms4 * TC + half * 512 + 512],
                                in0=hrel[half][:, :],
                                in1=gate_bc[:, half * 512:half * 512 + 512],
                                op=ALU.mult),
                             wait=[("cMOEV", P.n("cMOEV")),
                                   ("cGBC", 2 * (8 * w + e_i) + 2)],
                             inc=[("cHG", 1)])
                for m in range(8):
                    widx = P.n("cWC") // 2
                    P.op("sync", lambda e, e_i=e_i, mh=mh, m=m: e.dma_start(
                            out=wld[0].ap().rearrange("p (a m) -> p a m", a=4),
                            in_=w2.ap()[e_i, mh * 512:(mh + 1) * 512,
                                        m * 128:(m + 1) * 128]
                                .rearrange("(a p) m -> p a m", p=128)),
                         wait=[("cWC", max(0, 2 * widx - 1))],
                         inc=[("sW0", 16)])
                    P.op("scalar", lambda e, widx=widx: e.activation(
                            out=wbf[widx % 2][:, 0:512], in_=wld[0][:, :],
                            func=AF.Copy),
                         wait=[("sW0", P.n("sW0")),
                               ("cMOMM", max(0, P.n("cMOMM") - 2))],
                         inc=[("cWC", 1)])
                    P.op("scalar", lambda e: None, inc=[("cWC", 1)])
                    for half in range(2):
                        pb = (2 * m + half) % 2
                        with_b2 = (e_i == 0 and mh == 0)

                        def w2mm(e, widx=widx, half=half, pb=pb, m=m, wb2=with_b2):
                            r = None
                            for kt in range(4):
                                r = e.matmul(
                                    ps_m[pb][:, :],
                                    wbf[widx % 2][:, kt * 128:(kt + 1) * 128],
                                    hg_sb[:, kt * TC + half * 512:
                                          kt * TC + half * 512 + 512],
                                    start=(kt == 0), stop=(kt == 3 and not wb2))
                            if wb2:
                                r = e.matmul(
                                    ps_m[pb][:, :],
                                    b2p_sb[:, m * 128:(m + 1) * 128],
                                    gfbpad[:, half * 512:half * 512 + 512],
                                    start=False, stop=True)
                            return r
                        P.op("tensor", w2mm,
                             wait=[("cWC", 2 * widx + 2),
                                   ("cHG", hg_base + 7 + half),
                                   ("cMOEV", max(0, P.n("cMOEV") - 1))],
                             inc=[("cMOMM", 1)])
                        uidx = P.n("sFFA") // 16
                        evw2 = [("cMOMM", P.n("cMOMM"))]
                        if m % 2 == 0 and half == 0:
                            evw2.append(("sFFA", 16 * uidx))
                        P.op("scalar", lambda e, pb=pb, m=m, half=half: e.copy(
                                out=fstg[:, (m % 2) * 1024 + half * 512:
                                         (m % 2) * 1024 + half * 512 + 512],
                                in_=ps_m[pb][:, :]),
                             wait=evw2,
                             inc=[("cMOEV", 1)])
                        if m % 2 == 1 and half == 1:
                            first = (e_i == 0 and mh == 0)
                            P.op("gpsimd", lambda e, w=w, m=m, first=first:
                                 e.dma_start(
                                    out=ffn_d.ap()[w, (m - 1) * 128:(m + 1) * 128, :]
                                        .rearrange("(a p) t -> p a t", p=128),
                                    in_=fstg.ap().rearrange("p (a t) -> p a t", a=2),
                                    accum_op=(ALU.bypass if first else ALU.add)),
                                 wait=[("cMOEV", P.n("cMOEV"))],
                                 inc=[("sFFA", 16)])

        def wave_ln2(w):
            for tt in range(8):
                k = w * 8 + tt
                P.op("sync", lambda e, w=w, tt=tt: e.dma_start(
                        out=ffnt_t.ap().rearrange("p (a t) -> p a t", a=8),
                        in_=ffn_d.ap()[w, :, tt * 128:(tt + 1) * 128]
                            .rearrange("(a p) t -> p a t", p=128)),
                     wait=[("sFFA", 16 * 128 * (w + 1)), ("cS", P.n("cS")),
                           ("cSQR", P.n("cSQR"))],
                     inc=[("sLN2D", 16)])
                P.op("sync", lambda e, w=w, tt=tt: e.dma_start(
                        out=xres_t[:, :],
                        in_=x2tf_d.ap()[w, tt * 128:(tt + 1) * 128, :]),
                     wait=[("sX2G", 16 * 8 * (w + 1))] if tt == 0 else [],
                     inc=[("sLN2D", 16)])

                def ftr(e, base=P.n("sLN2D")):
                    r = None
                    for kt in range(8):
                        r = e.transpose(
                            ps_m[kt // 4][:, (kt % 4) * 128:(kt % 4 + 1) * 128],
                            ffnt_t[:, kt * 128:(kt + 1) * 128], identf_sb[:, :])
                    return r
                P.op("tensor", ftr,
                     wait=[("sLN2D", 16 * (P.n("sLN2D") // 16)),
                           ("cS", P.n("cS")), ("cMOEV", P.n("cMOEV"))],
                     inc=[("cGT", 1)])
                for hh in range(2):
                    P.op("vector", lambda e, hh=hh: e.tensor_tensor(
                            out=s_t[:, hh * 512:(hh + 1) * 512],
                            in0=ps_m[hh][:, :],
                            in1=xres_t[:, hh * 512:(hh + 1) * 512],
                            op=ALU.add),
                         wait=[("cGT", P.n("cGT")),
                               ("sLN2D", 16 * (P.n("sLN2D") // 16)),
                               ("cOUTT", P.n("cOUTT"))] if hh == 0 else
                              [("cGT", P.n("cGT"))],
                         inc=[("cS", 1)] if hh == 1 else [])
                ln_stats(s_t)
                P.op("vector", lambda e: e.tensor_scalar(
                        out=x2t[:, :], in0=s_t[:, :],
                        scalar1=lnst[:, 2:3], scalar2=lnst[:, 6:7],
                        op0=ALU.subtract, op1=ALU.mult),
                     wait=[("cLN", P.n("cLN")), ("sOUT", P.n("sOUT")),
                           ("sX2G", P.n("sX2G"))],
                     inc=[("cOUTT", 1)])
                P.op("scalar", lambda e, w=w, tt=tt: e.dma_start(
                        out=out_d.ap()[w * TC + tt * 128:
                                       w * TC + (tt + 1) * 128, :],
                        in_=x2t[:, :]),
                     wait=[("cOUTT", P.n("cOUTT"))],
                     inc=[("sOUT", 16)])

        front(0)
        front(1)
        P.op("gpsimd", lambda e: e.collective_compute(
                "AllReduce", ALU.add, replica_groups=RG,
                ins=[bcd_part.ap()[0].opt()], outs=[bcd_full.ap()[0].opt()]),
             wait=[("sBCDP", 16 * 2)], inc=[("ccBCD", 1)])

        for c in range(NCH):
            if 2 * c + 2 < NF:
                front(2 * c + 2)
            scan_chunk(c)
            if 2 * c + 3 < NF:
                front(2 * c + 3)
            if c + 1 < NCH:
                P.op("gpsimd", lambda e, c=c: e.collective_compute(
                        "AllReduce", ALU.add, replica_groups=RG,
                        ins=[bcd_part.ap()[c + 1].opt()],
                        outs=[bcd_full.ap()[c + 1].opt()]),
                     wait=[("sBCDP", 16 * (2 * c + 4))], inc=[("ccBCD", 1)])
            if c == 7:
                P.op("gpsimd", lambda e: e.collective_compute(
                        "ReduceScatter", ALU.add, replica_groups=RG,
                        ins=[ops_in.ap()[0].opt()], outs=[ops_out.ap()[0].opt()]),
                     wait=[("sOP0", 16 * 32)], inc=[("ccRS", 1)])
            if c == 8:
                wave_pre(0)
            if 9 <= c <= 15:
                wave_expert(0, c - 9)
        wave_expert(0, 7)
        wave_ln2(0)
        P.op("gpsimd", lambda e: e.collective_compute(
                "ReduceScatter", ALU.add, replica_groups=RG,
                ins=[ops_in.ap()[1].opt()], outs=[ops_out.ap()[1].opt()]),
             wait=[("sOP0", 16 * 64)], inc=[("ccRS", 1)])
        wave_pre(1)
        for e_i in range(NE):
            wave_expert(1, e_i)
        wave_ln2(1)
        P.op("gpsimd", lambda e: e.wait_ge(SM["sOUT"], 16 * 16))

        block = es.enter_context(nc.Block())

        def emit(eng_name):
            autodrain = eng_name in ("scalar", "vector")

            def build(eng):
                for (waits, fn, incs) in P.items[eng_name]:
                    if autodrain:
                        eng.drain()
                    for wi, (s, v) in enumerate(waits):
                        if wi and wi % 2 == 0:
                            eng.nop(nofuse=True)
                        eng.wait_ge(SM[s], v)
                    inst = fn(eng)
                    for ii, (s, k) in enumerate(incs):
                        if inst is None or ii > 0:
                            eng.sem_inc(SM[s], k)
                        else:
                            inst.then_inc(SM[s], k)
            return build

        global _LAST_PLAN
        _LAST_PLAN = P
        block.sync(emit("sync"))
        block.tensor(emit("tensor"))
        block.scalar(emit("scalar"))
        block.vector(emit("vector"))
        block.gpsimd(emit("gpsimd"))

    return nc


_LAST_PLAN = None
_NC_CACHE = None


def _get_nc():
    global _NC_CACHE
    if _NC_CACHE is None:
        _NC_CACHE = build_nc()
    return _NC_CACHE


def kernel(L0_attn_x, L0_attn_in_proj_w, L0_attn_conv_w, L0_attn_conv_b,
           L0_attn_x_proj_w, L0_attn_dt_w, L0_attn_dt_b, L0_attn_A_log,
           L0_attn_D, L0_attn_out_proj_w, L0_ffn_Wr, L0_ffn_br, L0_ffn_W1,
           L0_ffn_b1, L0_ffn_W2, L0_ffn_b2):
    x = np.ascontiguousarray(np.asarray(L0_attn_x, np.float32)[0])
    xtv = np.ascontiguousarray(x.T)
    W_in = np.asarray(L0_attn_in_proj_w, np.float32)
    conv_w = np.asarray(L0_attn_conv_w, np.float32)[:, 0, :]
    conv_b = np.asarray(L0_attn_conv_b, np.float32)
    W_xp = np.asarray(L0_attn_x_proj_w, np.float32)
    dt_w = np.asarray(L0_attn_dt_w, np.float32)
    dt_b = np.asarray(L0_attn_dt_b, np.float32)
    A_log = np.asarray(L0_attn_A_log, np.float32)
    Dp = np.asarray(L0_attn_D, np.float32)
    W_out = np.asarray(L0_attn_out_proj_w, np.float32)
    Wr = np.asarray(L0_ffn_Wr, np.float32)
    br = np.asarray(L0_ffn_br, np.float32)
    W1 = np.ascontiguousarray(np.asarray(L0_ffn_W1, np.float32))
    b1 = np.asarray(L0_ffn_b1, np.float32)
    W2 = np.ascontiguousarray(np.asarray(L0_ffn_W2, np.float32))
    b2 = np.asarray(L0_ffn_b2, np.float32)

    sel = np.zeros((48, 17 * 128), dtype=BF)
    for rrow in range(17):
        sel[rrow, rrow * 128:(rrow + 1) * 128] = np.ones(128, BF)
    for rrow in range(16):
        sel[32 + rrow, rrow * 128:(rrow + 1) * 128] = np.ones(128, BF)
    sele = np.zeros((128, NE * 128), dtype=BF)
    for e_i in range(NE):
        sele[e_i, e_i * 128:(e_i + 1) * 128] = np.ones(128, BF)
    ident = np.eye(128, dtype=np.float32).astype(BF)
    identf = np.eye(128, dtype=np.float32)
    cvec = np.zeros((128, 8), np.float32)
    cvec[:, CV_NEG1] = -1.0
    cvec[:, CV_INV1024] = 1.0 / 1024.0
    cvec[:, CV_EPS] = 1e-5
    cvec[:, CV_30] = 30.0
    cvec[:, CV_NEG1E30] = -1e30
    cvec[:, CV_ONE] = 1.0
    cvec[:, CV_NEGHALF] = -0.5

    b2p = np.zeros((128, DM), np.float32)
    b2p[0:NE, :] = b2

    in_maps = []
    for k in range(NCORE):
        sl = slice(k * DSL, (k + 1) * DSL)
        cols = np.r_[k * DSL:(k + 1) * DSL, DI + k * DSL: DI + (k + 1) * DSL]
        xres_k = np.ascontiguousarray(
            np.concatenate([x[k * TC:(k + 1) * TC],
                            x[(8 + k) * TC:(9 + k) * TC]], axis=0))
        alog_k = np.ascontiguousarray(
            A_log[sl].reshape(NDT, 128, DS).transpose(1, 0, 2).reshape(128, NDT * DS))

        wxp_k = np.zeros((DSL, 48), np.float32)
        wxp_k[:, 0:16] = W_xp[sl, 0:16]
        wxp_k[:, 16] = W_xp[sl, 32]
        wxp_k[:, 32:48] = W_xp[sl, 16:32]

        def vslice(v):
            return np.ascontiguousarray(v[sl].reshape(NDT, 128).T)
        convw_k = np.ascontiguousarray(
            conv_w[sl].reshape(NDT, 128, 4).transpose(1, 0, 2).reshape(128, NDT * 4))
        b1v = np.ascontiguousarray(
            b1.reshape(NE, 16, 128).transpose(2, 0, 1).reshape(128, NE * 16))
        im = {
            "xt": xtv,
            "xres": xres_k,
            "win": np.ascontiguousarray(W_in[:, cols]),
            "wxp": wxp_k,
            "wosl": np.ascontiguousarray(W_out[sl]),
            "alog": alog_k,
            "dtwi": vslice(dt_w),
            "dtbi": vslice(dt_b),
            "dveci": vslice(Dp),
            "convwi": convw_k,
            "convbi": vslice(conv_b),
            "wri": Wr,
            "brbi": np.ascontiguousarray(np.broadcast_to(br, (128, NE))),
            "b1i": b1v,
            "b2pi": b2p,
            "w1": W1,
            "w2": W2,
            "selc": sel,
            "selei": sele,
            "identi": ident,
            "identfi": identf,
            "cvecin": cvec,
        }
        in_maps.append(im)

    nc = _get_nc()
    res = run_bass_kernel_spmd(nc, in_maps, core_ids=list(range(NCORE)))
    out = np.empty((L, DM), np.float32)
    for k in range(NCORE):
        o = res.results[k]["out"]
        out[k * TC:(k + 1) * TC] = o[0:TC]
        out[(8 + k) * TC:(9 + k) * TC] = o[TC:2 * TC]
    return out[None]
